# version 01 (final)
# baseline (speedup 1.0000x reference)
import numpy as np
import concourse.bass as bass
import concourse.mybir as mybir
from concourse.bass_utils import run_bass_kernel_spmd
from contextlib import ExitStack

F32 = mybir.dt.float32
BF16 = mybir.dt.bfloat16
I16 = mybir.dt.int16
U16 = mybir.dt.uint16
AF = mybir.ActivationFunctionType
ALU = mybir.AluOpType
AX = mybir.AxisListType

S = 2048
D = 1024
NH = 16
DH = 64
IO = 3072
NT = S // 128
ND = D // 128
NIO = IO // 128
NPAIR = NH // 2
EPS = 1e-5
QA = 32767.0

ENGINES = ("sync", "gpsimd", "vector", "scalar", "tensor")


class Tick:
    __slots__ = ("sem", "v")
    def __init__(self, sem):
        self.sem = sem
        self.v = 0


class Prog:

    def __init__(self, ticks):
        self.ticks = ticks
        self.ops = {en: [] for en in ENGINES}
        self.all_ticks = []
        self._pending_bar = {}

    def barrier(self):
        snap = [(t, t.v) for t in self.all_ticks if t.v > 0]
        for en in ENGINES:
            self._pending_bar[en] = snap

    def op(self, engine, fn, waits=(), n=1, tick=None):
        tk = tick if tick is not None else self.ticks[engine]
        bar = self._pending_bar.pop(engine, ())
        w = [(t.sem, v) for (t, v) in bar]
        w += [(t.sem, v) for (t, v) in waits if v > 0]
        self.ops[engine].append((w, fn, tk.sem, n))
        tk.v += n
        return tk.v

    def dma(self, engine, out, in_, waits=(), sem=None):
        return self.op(engine, lambda h, o=out, i=in_: h.dma_start(out=o, in_=i),
                       waits=waits, n=16, tick=sem)

    def emit(self, block):
        for en in ENGINES:
            oplist = self.ops[en]
            if not oplist:
                continue

            def body(h, oplist=oplist):
                seen = {}
                for (waits, fn, sem, n) in oplist:
                    for (s_, v_) in waits:
                        if seen.get(s_.num, -1) < v_:
                            h.wait_ge(s_, v_)
                            seen[s_.num] = v_
                    fn(h).then_inc(sem, n)
            getattr(block, en)(body)


def build(stage="full", dbg=False, dbg2=False):
    nc = bass.Bass("TRN2", target_bir_lowering=False, debug=False)
    full = (stage == "full")

    x_in = nc.dram_tensor("x", [S, D], F32, kind="ExternalInput").ap()
    y_out = nc.dram_tensor("y", [S, D], F32, kind="ExternalOutput").ap()
    wq_t = nc.dram_tensor("wq_t", [D, D], F32, kind="ExternalInput").ap()
    wk_t = nc.dram_tensor("wk_t", [D, D], F32, kind="ExternalInput").ap()
    wv_t = nc.dram_tensor("wv_t", [D, D], F32, kind="ExternalInput").ap()
    wo_t = nc.dram_tensor("wo_t", [D, D], F32, kind="ExternalInput").ap()
    wup_n = nc.dram_tensor("wup_n", [IO, D], F32, kind="ExternalInput").ap()
    wup_t = nc.dram_tensor("wup_t", [D, IO], F32, kind="ExternalInput").ap()
    wgt_n = nc.dram_tensor("wgt_n", [IO, D], F32, kind="ExternalInput").ap()
    wgt_t = nc.dram_tensor("wgt_t", [D, IO], F32, kind="ExternalInput").ap()
    wdn_n = nc.dram_tensor("wdn_n", [D, IO], F32, kind="ExternalInput").ap()
    wdn_t = nc.dram_tensor("wdn_t", [IO, D], F32, kind="ExternalInput").ap()

    if dbg2:
        dbg = False
        d_catu2 = nc.dram_tensor("d_catu", [128, NPAIR * S], BF16, kind="ExternalOutput").ap()
        d_att2 = nc.dram_tensor("d_att", [S, D], F32, kind="ExternalOutput").ap()
        d_xlnT2 = nc.dram_tensor("d_xlnT", [128, ND * S], BF16, kind="ExternalOutput").ap()
        d_v2 = nc.dram_tensor("d_v", [128, NT * NH * 65], BF16, kind="ExternalOutput").ap()
        d_qk7 = nc.dram_tensor("d_qk7", [128, 2 * S], BF16, kind="ExternalOutput").ap()
    if dbg:
        d_xlnT = nc.dram_tensor("d_xlnT", [128, ND * S], BF16, kind="ExternalOutput").ap()
        d_v = nc.dram_tensor("d_v", [128, NT * NH * 65], BF16, kind="ExternalOutput").ap()
        d_q0 = nc.dram_tensor("d_q0", [128, S], BF16, kind="ExternalOutput").ap()
        d_k0 = nc.dram_tensor("d_k0", [128, S], BF16, kind="ExternalOutput").ap()
        d_pt0 = nc.dram_tensor("d_pt0", [128, 2048], BF16, kind="ExternalOutput").ap()
        d_catu = nc.dram_tensor("d_catu", [128, NPAIR * S], BF16, kind="ExternalOutput").ap()
        d_rc7 = nc.dram_tensor("d_rc7", [1, S], F32, kind="ExternalOutput").ap()
        d_wob = nc.dram_tensor("d_wob", [128, ND * D], BF16, kind="ExternalOutput").ap()
        d_att = nc.dram_tensor("d_att", [S, D], F32, kind="ExternalOutput").ap()
    rc_d = nc.dram_tensor("rc_d", [2, S], F32).ap()
    dram = {}
    if full:
        dram["gate_d"] = nc.dram_tensor("gate_d", [IO, S], F32).ap()
        dram["up_d"] = nc.dram_tensor("up_d", [IO, S], F32).ap()
        dram["g2u_d"] = nc.dram_tensor("g2u_d", [IO, S], F32).ap()
        dram["ou_d"] = nc.dram_tensor("ou_d", [IO, S], F32).ap()
        dram["x2_d"] = nc.dram_tensor("x2_d", [S, D], F32).ap()
        dram["wupq_d"] = nc.dram_tensor("wupq_d", [ND, 128, IO], BF16).ap()
        dram["wgtq_d"] = nc.dram_tensor("wgtq_d", [ND, 128, IO], BF16).ap()
        dram["wdnq_d"] = nc.dram_tensor("wdnq_d", [IO, D], BF16).ap()
        dram["st"] = {"up": nc.dram_tensor("stup_d", [32, IO], F32).ap(),
                      "gt": nc.dram_tensor("stgt_d", [32, IO], F32).ap(),
                      "dn": nc.dram_tensor("stdn_d", [96, D], F32).ap()}
        dram["iv"] = {"up": nc.dram_tensor("ivup_d", [32, IO], F32).ap(),
                      "gt": nc.dram_tensor("ivgt_d", [32, IO], F32).ap(),
                      "dn": nc.dram_tensor("ivdn_d", [96, D], F32).ap()}
        dram["cc_in"] = [nc.dram_tensor(f"cc_in{k}", [128, 128], F32).ap()
                         for k in range(4)]
        dram["cc_out"] = [nc.dram_tensor(f"cc_out{k}", [128, 128], F32).ap()
                          for k in range(4)]

    ctx = ExitStack()
    e = ctx.enter_context
    with ctx:
        qS = Tick(e(nc.semaphore("qS")))
        qG = Tick(e(nc.semaphore("qG")))
        tV = Tick(e(nc.semaphore("tV")))
        tA = Tick(e(nc.semaphore("tA")))
        tP = Tick(e(nc.semaphore("tP")))
        tG = Tick(e(nc.semaphore("tG")))
        ccs = [Tick(e(nc.semaphore(f"cc{k}"))) for k in range(4)] if full else []
        P = Prog({"sync": qS, "gpsimd": qG, "vector": tV,
                  "scalar": tA, "tensor": tP})
        xld = [Tick(e(nc.semaphore(f"xld{i}"))) for i in range(4)]
        pnl = [Tick(e(nc.semaphore(f"pnl{i}"))) for i in range(2)]
        ptro = [Tick(e(nc.semaphore(f"ptro{i}"))) for i in range(2)]
        usp = [Tick(e(nc.semaphore(f"usp{i}"))) for i in range(2)]
        pwl = [Tick(e(nc.semaphore(f"pwl{i}"))) for i in range(2)]
        pbl = [Tick(e(nc.semaphore(f"pbl{i}"))) for i in range(2)]
        psl = [Tick(e(nc.semaphore(f"psl{i}"))) for i in range(2)]
        pol = [Tick(e(nc.semaphore(f"pol{i}"))) for i in range(2)]
        x2s = [Tick(e(nc.semaphore(f"x2s{i}"))) for i in range(2)]
        scl = Tick(e(nc.semaphore("scl")))
        ccz = Tick(e(nc.semaphore("ccz")))
        wdl = Tick(e(nc.semaphore("wdl")))
        pql = [Tick(e(nc.semaphore(f"pql{i}"))) for i in range(2)]
        gsp = [Tick(e(nc.semaphore(f"gsp{i}"))) for i in range(2)]
        grl = [Tick(e(nc.semaphore(f"grl{i}"))) for i in range(2)]
        g2s = [Tick(e(nc.semaphore(f"g2s{i}"))) for i in range(2)]
        url = [Tick(e(nc.semaphore(f"url{i}"))) for i in range(2)]
        g2l = [Tick(e(nc.semaphore(f"g2l{i}"))) for i in range(2)]
        ous = [Tick(e(nc.semaphore(f"ous{i}"))) for i in range(2)]
        orl = [Tick(e(nc.semaphore(f"orl{i}"))) for i in range(2)]
        x2r = [Tick(e(nc.semaphore(f"x2r{i}"))) for i in range(2)]
        wvl = [Tick(e(nc.semaphore(f"wvl{i}"))) for i in range(2)]
        wol = [Tick(e(nc.semaphore(f"wol{i}"))) for i in range(2)]
        xsl = [Tick(e(nc.semaphore(f"xsl{i}"))) for i in range(2)]
        qkl = Tick(e(nc.semaphore("qkl")))
        rcl = Tick(e(nc.semaphore("rcl")))
        dmp = Tick(e(nc.semaphore("dmp")))
        yst = [Tick(e(nc.semaphore(f"yst{i}"))) for i in range(2)]
        P.all_ticks = ([qS, qG, tV, tA, tP, tG, qkl, rcl, dmp, scl,
                        ccz, wdl] + yst + ptro + usp + xld + wvl + wol + xsl +
                       pnl + pwl + pbl + psl + pol + x2s + pql + gsp + grl +
                       g2s + url + g2l + ous + orl + x2r)

        def gop(fn, waits=()):
            return P.op("gpsimd", fn, waits=waits, n=1, tick=tG)

        ps_sA = e(nc.psum_tensor("ps_sA", [128, 1024], F32))
        ps_sB = e(nc.psum_tensor("ps_sB", [128, 1024], F32))
        ps_o1 = e(nc.psum_tensor("ps_o1", [128, 512], F32))
        ps_o2 = e(nc.psum_tensor("ps_o2", [128, 512], F32))
        ps_x = e(nc.psum_tensor("ps_x", [128, 512], F32))
        ps_tr = e(nc.psum_tensor("ps_tr", [128, 512], BF16))

        tri = e(nc.sbuf_tensor("tri", [128, 128], BF16))
        trif = e(nc.sbuf_tensor("trif", [128, 128], F32))
        idnb = e(nc.sbuf_tensor("idnb", [128, 128], BF16))
        idnf = e(nc.sbuf_tensor("idnf", [128, 128], F32))
        ones_r = e(nc.sbuf_tensor("ones_r", [1, 128], F32))
        epsb = e(nc.sbuf_tensor("epsb", [128, 1], F32))
        zs = e(nc.sbuf_tensor("zs", [128, 128], F32))
        rstd_a = e(nc.sbuf_tensor("rstd_a", [128, NT], F32))
        negmr_a = e(nc.sbuf_tensor("negmr_a", [128, NT], F32))
        lntmp = e(nc.sbuf_tensor("lntmp", [128, NT], F32))
        bnst = e(nc.sbuf_tensor("bnst", [128, NT * 12], F32))
        mv_a = e(nc.sbuf_tensor("mv_a", [128, NT * 2], F32))
        catT = e(nc.sbuf_tensor("catT", [128, NPAIR * S], BF16))
        x_lnT = e(nc.sbuf_tensor("x_lnT", [128, ND * S], BF16))
        xm_row = e(nc.sbuf_tensor("xm_row", [1, 128], F32))
        sc1 = e(nc.sbuf_tensor("sc1", [1, 24], F32))
        scale_bc = e(nc.sbuf_tensor("scale_bc", [128, 16], F32))

        t0 = gop(lambda g: g.memset(trif[:], 1.0))
        gop(lambda g: g.affine_select(out=trif[:], in_=trif[:],
                                      compare_op=ALU.is_ge, fill=0.0, base=0,
                                      pattern=[[1, 128]], channel_multiplier=-1),
            waits=[(tG, t0)])
        t1 = gop(lambda g: g.memset(idnf[:], 1.0))
        t_consts = gop(lambda g: g.affine_select(out=idnf[:], in_=idnf[:],
                                                 compare_op=ALU.is_equal,
                                                 fill=0.0, base=0,
                                                 pattern=[[1, 128]],
                                                 channel_multiplier=-1),
                       waits=[(tG, t1)])
        gop(lambda g: g.memset(ones_r[:], 1.0))
        t_epsb = gop(lambda g: g.memset(epsb[:], EPS))
        t_ones = gop(lambda g: g.memset(zs[:], 0.0))

        t_tri_bf = P.op("vector", lambda v: v.tensor_copy(tri[:], trif[:]),
                        waits=[(tG, t_consts)])
        t_idn_bf = P.op("vector", lambda v: v.tensor_copy(idnb[:], idnf[:]),
                        waits=[(tG, t_consts)])

        scA = ExitStack()
        eA = scA.enter_context
        x_sb = eA(nc.sbuf_tensor("x_sb", [128, NT * D], F32))
        xln_tok = eA(nc.sbuf_tensor("xln_tok", [128, NT * D], BF16))

        def xload(t, waits=()):
            return P.dma("sync", x_sb[:, t * D:(t + 1) * D],
                         x_in[t * 128:(t + 1) * 128, :], waits=waits,
                         sem=xld[t % 4])
        t_xload = [xload(t) for t in range(4)]
        for t in range(NT):
            c0 = t * D
            P.op("vector", lambda v, t=t, c0=c0: v.bn_stats(
                bnst[:, t * 12:t * 12 + 6], x_sb[:, c0:c0 + 512]),
                waits=[(xld[t % 4], t_xload[t])])
            tb = P.op("vector", lambda v, t=t, c0=c0: v.bn_stats(
                bnst[:, t * 12 + 6:t * 12 + 12], x_sb[:, c0 + 512:c0 + 1024]))
            P.op("vector", lambda v, t=t: v.bn_aggr(
                mv_a[:, 2 * t:2 * t + 2], bnst[:, t * 12:t * 12 + 12]),
                waits=[(tV, tb)])
            if t + 4 < NT:
                t_xload.append(xload(t + 4, waits=[(tV, tb)]))
        t_bn = tV.v

        t_ln1 = P.op("scalar", lambda a: a.activation(
            lntmp[:], mv_a[:, 1:2 * NT:2], AF.Ln, bias=epsb[:], scale=1.0),
            waits=[(tV, t_bn), (tG, t_epsb)])
        t_rstd = P.op("scalar", lambda a: a.activation(
            rstd_a[:], lntmp[:], AF.Exp, bias=0.0, scale=-0.5),
            waits=[(tA, t_ln1)])
        t_negmr = P.op("vector", lambda v: v.scalar_tensor_tensor(
            negmr_a[:], mv_a[:, 0:2 * NT:2], -1.0, rstd_a[:],
            ALU.mult, ALU.mult), waits=[(tA, t_rstd)])

        t_xln = [P.op("vector", lambda v, t=t: v.tensor_scalar(
            xln_tok[:, t * D:(t + 1) * D], x_sb[:, t * D:(t + 1) * D],
            rstd_a[:, t:t + 1], negmr_a[:, t:t + 1], ALU.mult, ALU.add),
            waits=[(tV, t_negmr)]) for t in range(NT)]

        def pe_guard(waits=()):
            return P.op("tensor", lambda tt: tt.matmul(
                ps_x[0:1, 0:1], ones_r[0:1, 0:1], ones_r[0:1, 0:1],
                start=True, stop=True), waits=waits)

        t_trA = {}
        prev_evac = 0
        for t in range(NT):
            for half in range(2):
                w = [(tV, t_xln[t]), (tV, t_idn_bf), (tA, prev_evac),
                     (tG, t_ones)]
                for i4 in range(4):
                    i = half * 4 + i4
                    tk = P.op("tensor", lambda tt, t=t, i=i, i4=i4: tt.transpose(
                        ps_tr[:, i4 * 128:(i4 + 1) * 128],
                        xln_tok[:, t * D + i * 128: t * D + (i + 1) * 128],
                        idnb[:]), waits=w)
                tk = pe_guard()
                prev_evac = P.op("scalar", lambda a, t=t, half=half: a.copy(
                    x_lnT.rearrange("p (i n) -> p i n", n=S)[
                        :, half * 4:half * 4 + 4, t * 128:(t + 1) * 128],
                    ps_tr.rearrange("p (i n) -> p i n", n=128)),
                    waits=[(tP, tk)])
                t_trA[(t, half)] = prev_evac
        t_xlnT = prev_evac
        if dbg:
            P.dma("gpsimd", d_xlnT, x_lnT[:], waits=[(tA, t_xlnT)], sem=dmp)
        scA.close()
        P.barrier()

        scPre = ExitStack()
        if full:
            ePre = scPre.enter_context
            wnst = ePre(nc.sbuf_tensor("wnst", [128, 2 * 1536], F32))
            s_nat = ePre(nc.sbuf_tensor("s_nat", [128, 768], F32))
            inv_nat = ePre(nc.sbuf_tensor("inv_nat", [128, 768], F32))
            evb = ePre(nc.sbuf_tensor("evb", [128, 2 * 512], F32))
        scB = ExitStack()
        eB = scB.enter_context
        v_sb = eB(nc.sbuf_tensor("v_sb", [128, NT * NH * 65], BF16))
        wst = eB(nc.sbuf_tensor("wst", [128, 2 * 512], F32))
        wvb = eB(nc.sbuf_tensor("wvb", [128, ND * D], BF16))
        qkst = eB(nc.sbuf_tensor("qkst", [128, 2 * ND * 128], F32))
        wqb = eB(nc.sbuf_tensor("wqb", [128, ND * 128], BF16))
        wkb = eB(nc.sbuf_tensor("wkb", [128, ND * 128], BF16))
        q_pair = eB(nc.sbuf_tensor("q_pair", [128, S], BF16))
        k_pair = eB(nc.sbuf_tensor("k_pair", [128, S], BF16))
        pTA = eB(nc.sbuf_tensor("pTA", [128, 2048], BF16))
        pTB = eB(nc.sbuf_tensor("pTB", [128, 2048], BF16))
        lnb2 = eB(nc.sbuf_tensor("lnb2", [65, S], F32))
        rc2 = eB(nc.sbuf_tensor("rc2", [65, S], F32))
        lnbufA = lnb2[0:1, :]
        lnbufB = lnb2[64:65, :]
        rcA = rc2[0:1, :]
        rcB = rc2[64:65, :]
        rc_bc = eB(nc.sbuf_tensor("rc_bc", [128, 512], F32))
        gdst = eB(nc.sbuf_tensor("gdst", [1, 4], F32))
        v_sb4 = v_sb.rearrange("p (t h c) -> p t h c", h=NH, c=65)

        t_wv_cast = []
        for cq in range(2 * ND):
            c, j = divmod(cq, 2)
            sl = cq % 2
            w = [(tV, t_wv_cast[cq - 2])] if cq >= 2 else []
            tl = P.dma("sync", wst[:, sl * 512:(sl + 1) * 512],
                       wv_t[c * 128:(c + 1) * 128, j * 512:(j + 1) * 512],
                       waits=w, sem=wvl[sl])
            t_wv_cast.append(P.op("vector", lambda v, c=c, j=j, sl=sl:
                                  v.tensor_copy(
                wvb[:, c * D + j * 512:c * D + (j + 1) * 512],
                wst[:, sl * 512:(sl + 1) * 512]), waits=[(wvl[sl], tl)]))
        t_vones = P.op("vector", lambda v: v.memset(v_sb4[:, :, :, 64:65], 1.0))

        t_vevac = {}
        pend = None
        for t in range(NT):
            for j in range(2):
                ps = ps_o1 if j == 0 else ps_o2
                w = [(tA, t_trA[(t, 1)]), (tA, t_trA[(t, 0)]),
                     (tV, t_wv_cast[-1]), (tA, t_xlnT)]
                if t > 0:
                    w.append((tA, t_vevac.get((t - 1, j), 0)))
                for c in range(ND):
                    tk = P.op("tensor", lambda tt, t=t, j=j, c=c, ps=ps:
                              tt.matmul(
                        ps[:], x_lnT[:, c * S + t * 128: c * S + (t + 1) * 128],
                        wvb[:, c * D + j * 512: c * D + (j + 1) * 512],
                        start=(c == 0), stop=(c == ND - 1)),
                        waits=w if c == 0 else ())
                if pend is not None:
                    (pt, pj, pps) = pend
                    t_vevac[(pt, pj)] = P.op(
                        "scalar", lambda a, pt=pt, pj=pj, pps=pps: a.copy(
                            v_sb4[:, pt, 8 * pj:8 * (pj + 1), 0:64],
                            pps.rearrange("p (h n) -> p h n", n=64)),
                        waits=[(tP, tk)])
                pend = (t, j, ps)
        gk_ = pe_guard()
        (pt, pj, pps) = pend
        t_vevac[(pt, pj)] = P.op(
            "scalar", lambda a, pt=pt, pj=pj, pps=pps: a.copy(
                v_sb4[:, pt, 8 * pj:8 * (pj + 1), 0:64],
                pps.rearrange("p (h n) -> p h n", n=64)),
            waits=[(tP, gk_)])
        t_vdone = tA.v
        if dbg:
            P.dma("gpsimd", d_v, v_sb[:], waits=[(tA, t_vdone), (tV, t_vones)], sem=dmp)

        vunits = []
        t_prep1 = 0
        st1_done = {}
        deq_out = {}
        if full:
            wqst = eB(nc.sbuf_tensor("wqst", [128, 2 * 512], F32))
            ivbcb = eB(nc.sbuf_tensor("ivbcb", [128, 2 * 512], F32))
            sbcb = eB(nc.sbuf_tensor("sbcb", [128, 2 * 512], F32))
            wi16b = eB(nc.sbuf_tensor("wi16b", [128, 2 * 512], I16))
            wbfb = eB(nc.sbuf_tensor("wbfb", [128, 2 * 512], BF16))
            for k in range(4):
                P.dma("gpsimd", dram["cc_in"][k], zs[:],
                      waits=[(tG, t_ones)], sem=ccz)

            t_red = {}
            prev_done = None
            prep_specs = [("gt", wgt_n, dram["wgtq_d"], NIO, D, 32),
                          ("up", wup_n, dram["wupq_d"], NIO, D, 32),
                          ("dn", wdn_n, dram["wdnq_d"], ND, IO, 96)]
            for wname, nat, _, nchunks, I_, bW in prep_specs:
                nb = I_ // 32
                nhalf = 2 if I_ > 1536 else 1
                hw_ = I_ // nhalf
                nbh = nb // nhalf
                for c_ in range(nchunks):
                    for h_ in range(nhalf):
                        u = c_ * nhalf + h_
                        sl = u % 2
                        w = []
                        if u >= 2:
                            w.append((tV, t_red[(wname, u - 2)]))
                        if u < 2 and prev_done is not None:
                            w.extend(prev_done)
                        tl = P.dma("sync", wnst[:, sl * 1536: sl * 1536 + hw_],
                                   nat[c_ * 128:(c_ + 1) * 128,
                                       h_ * hw_:(h_ + 1) * hw_], waits=w,
                                   sem=pnl[sl])
                        t_red[(wname, u)] = P.op(
                            "vector", lambda v, c_=c_, h_=h_, sl=sl, hw_=hw_,
                            nb=nb, nbh=nbh: v.reduce_max(
                                s_nat.rearrange("p (a b) -> p a b", b=1)[
                                    :, c_ * nb + h_ * nbh:
                                    c_ * nb + h_ * nbh + nbh, :],
                                wnst[:, sl * 1536: sl * 1536 + hw_].rearrange(
                                    "p (a b) -> p a b", b=32),
                                AX.X, apply_absolute_value=True),
                            waits=[(pnl[sl], tl)])
                nu = nchunks * nhalf
                ncols = nchunks * nb
                tts = P.op("vector", lambda v, ncols=ncols: v.tensor_scalar(
                    s_nat[:, 0:ncols], s_nat[:, 0:ncols], 1.0 / 7.0, 1e-12,
                    ALU.mult, ALU.max), waits=[(tV, t_red[(wname, nu - 1)])])
                tiv = P.op("vector", lambda v, ncols=ncols: v.reciprocal(
                    inv_nat[:, 0:ncols], s_nat[:, 0:ncols]), waits=[(tV, tts)])
                t_out = {}
                bidx = 0
                for (src, psq, dst) in ((s_nat, ps_sA, dram["st"][wname]),
                                        (inv_nat, ps_sB, dram["iv"][wname])):
                    prev_ev = 0
                    nbat = (nchunks + 3) // 4
                    for b_ in range(nbat):
                        lo = 4 * b_
                        hi = min(lo + 4, nchunks)
                        esl = bidx % 2
                        w = [(tV, tiv), (tA, prev_ev), (tG, t_consts)]
                        if bidx >= 2:
                            w.append((ptro[esl], t_out[bidx - 2]))
                        for m, c_ in enumerate(range(lo, hi)):
                            tk = P.op("tensor", lambda tt, m=m, c_=c_, nb=nb,
                                      psq=psq, src=src, bW=bW: tt.transpose(
                                psq[0:bW, m * 128:(m + 1) * 128],
                                src[:, c_ * nb:(c_ + 1) * nb], idnf[:]),
                                waits=w)
                        tk = pe_guard()
                        prev_ev = P.op("scalar", lambda a, esl=esl, lo=lo,
                                       hi=hi, psq=psq, bW=bW: a.copy(
                            evb[0:bW, esl * 512: esl * 512 + (hi - lo) * 128],
                            psq[0:bW, 0:(hi - lo) * 128]), waits=[(tP, tk)])
                        t_out[bidx] = P.dma(
                            "sync", dst[:, lo * 128: hi * 128],
                            evb[0:bW, esl * 512: esl * 512 + (hi - lo) * 128],
                            waits=[(tA, prev_ev)], sem=ptro[esl])
                        bidx += 1
                st1_done[wname] = [(ptro[0], ptro[0].v),
                                   (ptro[1], ptro[1].v)]
                prev_done = st1_done[wname]
            t_prep1 = tA.v

            dstate = {"sl": 0, "t1": [0, 0], "t2": [0, 0], "to": [0, 0],
                      "tw": [0, 0], "tb": [0, 0], "ts": [0, 0]}
            deq_specs = [("gt", wgt_t, dram["wgtq_d"], dram["st"]["gt"],
                          dram["iv"]["gt"], ND, 512, 6),
                         ("up", wup_t, dram["wupq_d"], dram["st"]["up"],
                          dram["iv"]["up"], ND, 512, 6),
                         ("dn", wdn_t, dram["wdnq_d"], dram["st"]["dn"],
                          dram["iv"]["dn"], NIO, 512, 2)]

            def mk_unit(wname, wT, outd, stD, ivD, c_, q_, qw):
                def fn():
                    sl = dstate["sl"]
                    dstate["sl"] ^= 1
                    so = sl * 512
                    ww = list(st1_done[wname]) + [(tV, dstate["t1"][sl])]
                    tw = P.dma("sync", wqst[:, so:so + qw],
                               wT[c_ * 128:(c_ + 1) * 128, q_ * qw:(q_ + 1) * qw],
                               waits=ww, sem=pwl[sl])
                    dstate["tw"][sl] = tw
                    tb = P.dma("sync", ivbcb[:, so:so + qw],
                               ivD[4 * c_:4 * c_ + 4, q_ * qw:(q_ + 1) * qw]
                               .unsqueeze(1).broadcast_to([4, 32, qw]),
                               waits=[(tV, dstate["t1"][sl])], sem=pbl[sl])
                    ts_ = P.dma("sync", sbcb[:, so:so + qw],
                                stD[4 * c_:4 * c_ + 4, q_ * qw:(q_ + 1) * qw]
                                .unsqueeze(1).broadcast_to([4, 32, qw]),
                                waits=[(tV, dstate["t2"][sl])], sem=psl[sl])
                    t1 = P.op("vector", lambda v: v.tensor_tensor(
                        wi16b[:, so:so + qw], wqst[:, so:so + qw],
                        ivbcb[:, so:so + qw], ALU.mult),
                        waits=[(pwl[sl], tw), (pbl[sl], tb),
                               (tV, dstate["t2"][sl])])
                    dstate["t1"][sl] = t1
                    t2 = P.op("vector", lambda v: v.tensor_tensor(
                        wbfb[:, so:so + qw], wi16b[:, so:so + qw],
                        sbcb[:, so:so + qw], ALU.mult),
                        waits=[(tV, t1), (psl[sl], ts_),
                               (pol[sl], dstate["to"][sl])])
                    dstate["t2"][sl] = t2
                    if wname == "dn":
                        dst = outd[c_ * 128:(c_ + 1) * 128,
                                   q_ * qw:(q_ + 1) * qw]
                    else:
                        dst = outd[c_][:, q_ * qw:(q_ + 1) * qw]
                    to = P.dma("sync", dst, wbfb[:, so:so + qw],
                               waits=[(tV, t2)], sem=pol[sl])
                    dstate["to"][sl] = to
                    deq_out[(wname, c_, q_)] = (pol[sl], to)
                return fn

            for wname, wT, outd, stD, ivD, nch, qw, nq in deq_specs:
                for c_ in range(nch):
                    for q_ in range(nq):
                        vunits.append((2 * qw,
                                       mk_unit(wname, wT, outd, stD, ivD,
                                               c_, q_, qw)))
        unit_idx = [0]

        def drain_units(budget):
            spent = 0
            while unit_idx[0] < len(vunits) and spent < budget:
                cost, fn = vunits[unit_idx[0]]
                fn()
                spent += cost
                unit_idx[0] += 1

        t_mask = {}
        t_av = {}
        sc_t = {}
        exp_t = {}
        msk_last = 0
        prev_exp = {0: 0, 1: 0}
        t_av_done = []
        t_qk_evac = {}
        t_qkproj_last = 0
        t_qk_cast_prev = (0, 0)
        t_rc_prev = 0
        t_rcin = 0
        t_rcout = 0
        prev_nrm = 0

        for p in range(NPAIR):
            w = [(tV, t_qk_cast_prev[1])] if p > 0 else []
            tl_q = P.dma("gpsimd",
                         qkst[:, 0:ND * 128].rearrange("p (c n) -> p c n", n=128),
                         wq_t.rearrange("(c p) n -> p c n", p=128)[
                             :, :, 128 * p:128 * (p + 1)], waits=w, sem=qkl)
            tl_k = P.dma("gpsimd",
                         qkst[:, ND * 128:].rearrange("p (c n) -> p c n", n=128),
                         wk_t.rearrange("(c p) n -> p c n", p=128)[
                             :, :, 128 * p:128 * (p + 1)], sem=qkl)
            wc = [(qkl, tl_k)]
            if p > 0:
                wc.append((tP, t_qkproj_last))
            c1 = P.op("vector", lambda v: v.tensor_copy(wqb[:], qkst[:, 0:ND * 128]),
                      waits=wc)
            c2 = P.op("vector", lambda v: v.tensor_copy(wkb[:], qkst[:, ND * 128:]),
                      waits=wc)
            t_qk_cast_prev = (c1, c2)

            pendq = None
            for idx, (dst, wb, psq) in enumerate(
                    ((q_pair, wqb, ps_sA), (k_pair, wkb, ps_sB))):
                for piece in range(4):
                    half = piece % 2
                    w2 = [(tV, c2), (tA, t_xlnT), (tA, t_prep1)]
                    if p > 0 and piece < 2:
                        w2.append((tA, prev_exp[idx]))
                    if piece >= 2:
                        w2.append((tV, t_qk_evac[(p, idx, piece - 2)]))
                    for c in range(ND):
                        tk = P.op("tensor", lambda tt, wb=wb, psq=psq, c=c,
                                  piece=piece, half=half: tt.matmul(
                            psq[:, half * 512:(half + 1) * 512],
                            wb[:, c * 128:(c + 1) * 128],
                            x_lnT[:, c * S + piece * 512:c * S + (piece + 1) * 512],
                            start=(c == 0), stop=(c == ND - 1)),
                            waits=w2 if c == 0 else ())
                    if pendq is not None:
                        (pdst, ppsq, ppiece, phalf, pidx) = pendq
                        t_qk_evac[(p, pidx, ppiece)] = P.op(
                            "vector", lambda v, pdst=pdst, ppsq=ppsq,
                            ppiece=ppiece, phalf=phalf: v.tensor_copy(
                                pdst[:, ppiece * 512:(ppiece + 1) * 512],
                                ppsq[:, phalf * 512:(phalf + 1) * 512]),
                            waits=[(tP, tk)])
                    pendq = (dst, psq, piece, half, idx)
            gq_ = pe_guard()
            (pdst, ppsq, ppiece, phalf, pidx) = pendq
            t_qk_evac[(p, pidx, ppiece)] = P.op(
                "vector", lambda v, pdst=pdst, ppsq=ppsq, ppiece=ppiece,
                phalf=phalf: v.tensor_copy(
                    pdst[:, ppiece * 512:(ppiece + 1) * 512],
                    ppsq[:, phalf * 512:(phalf + 1) * 512]),
                waits=[(tP, gq_)])
            t_qkproj_last = tP.v
            t_qk_done = t_qk_evac[(p, 1, 3)]
            if dbg and p == 0:
                P.dma("gpsimd", d_q0, q_pair[:], waits=[(tV, t_qk_done)], sem=dmp)
                P.dma("gpsimd", d_k0, k_pair[:], waits=[(tV, t_qk_done)], sem=dmp)

            for i4 in range(4):
                nj = 4 * i4 + 4
                njp = nj // 2
                for st in range(njp + 1):
                    if st < njp:
                        j0, j1 = 2 * st, 2 * st + 1
                        slot = st % 2
                        for (hh, psq) in ((0, ps_sA), (1, ps_sB)):
                            rows = slice(0, 64) if hh == 0 else slice(64, 128)
                            w3 = [(tV, t_qk_done)]
                            if i4 == 0:
                                w3.append((tV, t_qk_evac[(p, hh, 2)]))
                                w3.append((tV, t_qk_evac[(p, hh, 3)]))
                            if st >= 1:
                                w3.append((tA, exp_t[(st - 1, hh)]))
                            else:
                                w3.append((tA, prev_exp[hh]))
                            for j in (j0, j1):
                                r = j - 4 * i4
                                off = 128 * r if r >= 0 else 0
                                half = j % 2
                                qs = 512 * i4 + off
                                tk = P.op("tensor", lambda tt, j=j, off=off,
                                          half=half, qs=qs, psq=psq, rows=rows,
                                          hh=hh: tt.matmul(
                                    psq[:, half * 512 + off:(half + 1) * 512],
                                    k_pair[rows, 128 * j:128 * (j + 1)],
                                    q_pair[rows, qs:qs + 512 - off],
                                    start=True, stop=True,
                                    tile_position=(0, 0) if hh == 0 else (64, 0)),
                                    waits=w3)
                            sc_t[(st, hh)] = tk
                        g = pe_guard()
                        for (hh, psq, ptb) in ((0, ps_sA, pTA), (1, ps_sB, pTB)):
                            wexp = [(tP, g), (tV, msk_last)]
                            exp_t[(st, hh)] = P.op(
                                "scalar", lambda a, psq=psq, ptb=ptb, slot=slot:
                                a.activation(
                                    ptb[:, slot * 1024:(slot + 1) * 1024],
                                    psq[:], AF.Exp, bias=0.0, scale=0.125),
                                waits=wexp)
                        diag = (2 * st + 1 - 4 * i4) >= 0
                        if diag:
                            ga = P.op("scalar", lambda a: a.copy(
                                gdst[0:1, 0:1], zs[0:1, 0:1]))
                            for j in (j0, j1):
                                r = j - 4 * i4
                                if r < 0:
                                    continue
                                off = 128 * r
                                half = j % 2
                                for (hh, ptb) in ((0, pTA), (1, pTB)):
                                    lo = slot * 1024 + half * 512 + off
                                    t_mask[(p, i4, j, hh)] = P.op(
                                        "vector", lambda v, ptb=ptb, lo=lo:
                                        v.tensor_tensor(
                                            ptb[:, lo:lo + 128],
                                            ptb[:, lo:lo + 128],
                                            tri[:], ALU.mult),
                                        waits=[(tA, ga), (tV, t_tri_bf)])
                                    msk_last = t_mask[(p, i4, j, hh)]
                    if st >= 1:
                        sa = st - 1
                        for j in (2 * sa, 2 * sa + 1):
                            r = j - 4 * i4
                            off = 128 * r if r >= 0 else 0
                            half = j % 2
                            slot_a = sa % 2
                            for (hh, ptb, ps_o) in ((0, pTA, ps_o1),
                                                    (1, pTB, ps_o2)):
                                if r >= 0:
                                    w4 = [(tV, t_mask[(p, i4, j, hh)])]
                                else:
                                    w4 = [(tA, exp_t[(st, hh)] if st < njp
                                           else exp_t[(sa, hh)])]
                                if j == 0:
                                    w4.append((tA, t_vdone))
                                    w4.append((tV, t_vones))
                                    if t_av_done:
                                        w4.append((tA, t_av_done[-1]))
                                vcol = j * NH * 65 + (2 * p + hh) * 65
                                t_av[(p, i4, j, hh)] = P.op(
                                    "tensor", lambda tt, j=j, off=off,
                                    half=half, ps_o=ps_o, ptb=ptb, vcol=vcol,
                                    slot_a=slot_a, nj=nj: tt.matmul(
                                        ps_o[0:65, off:512],
                                        v_sb[:, vcol:vcol + 65],
                                        ptb[:, slot_a * 1024 + half * 512 + off:
                                            slot_a * 1024 + (half + 1) * 512],
                                        start=(j == 0), stop=(j == nj - 1)),
                                    waits=w4)
                prev_exp = {0: exp_t[(njp - 1, 0)], 1: exp_t[(njp - 1, 1)]}
                gav = pe_guard()
                for (hh, ps_o, lnb) in ((0, ps_o1, lnbufA), (1, ps_o2, lnbufB)):
                    col = i4 * 512
                    rows = slice(0, 64) if hh == 0 else slice(64, 128)
                    wl = [(tP, gav)]
                    if p > 0:
                        wl.append((tA, t_rc_prev))
                    P.op("scalar", lambda a, ps_o=ps_o, lnb=lnb, col=col:
                         a.activation(lnb[0:1, col:col + 512], ps_o[64:65, :],
                                      AF.Ln, bias=0.0, scale=1.0), waits=wl)
                    t_av_done.append(P.op(
                        "scalar", lambda a, ps_o=ps_o, col=col, rows=rows,
                        pcol=p * S: a.copy(catT[rows, pcol + col:pcol + col + 512],
                                           ps_o[0:64, :])))
                drain_units(8000)
                if dbg and p == 0 and i4 == 0:
                    P.dma("gpsimd", d_pt0, pTA[:],
                          waits=[(tV, t_mask[(0, 0, 3, 0)])], sem=dmp)
            wex = [(tA, t_av_done[-1])]
            if t_rcout:
                wex.append((rcl, t_rcout))
            P.op("scalar", lambda a: a.activation(
                rcA[:], lnbufA[:], AF.Exp, bias=0.0, scale=-1.0), waits=wex)
            t_rc_prev = P.op("scalar", lambda a: a.activation(
                rcB[:], lnbufB[:], AF.Exp, bias=0.0, scale=-1.0))
            wrc = [(tA, t_rc_prev)]
            if t_rcin:
                wrc.append((rcl, t_rcin))
            P.dma("gpsimd", rc_d[0:1, :], rcA[:], waits=wrc, sem=rcl)
            t_rcout = P.dma("gpsimd", rc_d[1:2, :], rcB[:], sem=rcl)
            for i4 in range(4):
                col = i4 * 512
                w = [(rcl, t_rcout), (tV, prev_nrm)]
                P.dma("gpsimd", rc_bc[0:64, :],
                      rc_d[0:1, col:col + 512].broadcast_to([64, 512]),
                      waits=w, sem=rcl)
                t_rcin = P.dma("gpsimd", rc_bc[64:128, :],
                               rc_d[1:2, col:col + 512].broadcast_to([64, 512]),
                               sem=rcl)
                prev_nrm = P.op("vector", lambda v, col=col, pcol=p * S:
                                v.tensor_tensor(
                    catT[:, pcol + col:pcol + col + 512],
                    catT[:, pcol + col:pcol + col + 512], rc_bc[:],
                    ALU.mult), waits=[(rcl, t_rcin)])
        t_cat_norm = prev_nrm
        if dbg:
            P.dma("gpsimd", d_catu, catT[:], waits=[(tV, t_cat_norm)], sem=dmp)
            P.dma("gpsimd", d_rc7, rcA[:], waits=[(tV, t_cat_norm)], sem=dmp)
        if dbg2:
            P.dma("gpsimd", d_catu2, catT[:], waits=[(tV, t_cat_norm)], sem=dmp)
            P.dma("gpsimd", d_xlnT2, x_lnT[:], waits=[(tV, t_cat_norm)], sem=dmp)
            P.dma("gpsimd", d_v2, v_sb[:], waits=[(tV, t_cat_norm)], sem=dmp)
            P.dma("gpsimd", d_qk7[:, 0:S], q_pair[:], waits=[(tV, t_cat_norm)], sem=dmp)
            P.dma("gpsimd", d_qk7[:, S:2 * S], k_pair[:], waits=[(tV, t_cat_norm)], sem=dmp)

        drain_units(10 ** 9)
        scB.close()
        scPre.close()
        P.barrier()

        scC = ExitStack()
        eC = scC.enter_context
        x2_sb = eC(nc.sbuf_tensor("x2_sb", [128, NT * D], F32))
        wob = eC(nc.sbuf_tensor("wob", [128, ND * D], BF16))
        wost = eC(nc.sbuf_tensor("wost", [128, 2 * D], F32))
        xst = eC(nc.sbuf_tensor("xst", [128, 2 * D], F32))
        xadj = eC(nc.sbuf_tensor("xadj", [128, 2 * D], F32))
        attb = eC(nc.sbuf_tensor("attb", [128, 2 * D], F32))
        ln2s = eC(nc.sbuf_tensor("ln2s", [128, 10], F32))

        t_wo_cast = []
        for c in range(ND):
            sl = c % 2
            w = [(tV, t_wo_cast[c - 2])] if c >= 2 else []
            tl = P.dma("sync", wost[:, sl * D:(sl + 1) * D],
                       wo_t[c * 128:(c + 1) * 128, :], waits=w, sem=wol[sl])
            t_wo_cast.append(P.op("vector", lambda v, c=c, sl=sl: v.tensor_copy(
                wob[:, c * D:(c + 1) * D], wost[:, sl * D:(sl + 1) * D]),
                waits=[(wol[sl], tl)]))

        if dbg:
            P.dma("gpsimd", d_wob, wob[:], waits=[(tV, t_wo_cast[-1])], sem=dmp)
        t_x2 = []
        hist = {}
        for t in range(NT):
            sl = t % 2
            so = sl * 5
            w = [(tV, t_x2[t - 2])] if t >= 2 else []
            tlx = P.dma("sync", xst[:, sl * D:(sl + 1) * D],
                        x_in[t * 128:(t + 1) * 128, :], waits=w, sem=xsl[sl])
            banks = ((ps_o1[:, :], ps_o2[:, :]) if sl == 0
                     else (ps_sA[:, 0:512], ps_sA[:, 512:1024]))
            tk_piece = []
            for j in range(2):
                w2 = [(tV, t_cat_norm), (tV, t_wo_cast[-1])]
                if t >= 2:
                    w2.append((tA, hist[("evc", t - 2, j)]))
                for c in range(ND):
                    tk = P.op("tensor", lambda tt, t=t, j=j, c=c, b=banks[j]:
                              tt.matmul(
                        b, catT[:, c * S + t * 128: c * S + (t + 1) * 128],
                        wob[:, c * D + j * 512: c * D + (j + 1) * 512],
                        start=(c == 0), stop=(c == ND - 1)),
                        waits=w2 if c == 0 else ())
                tk_piece.append(tk)
            gp_ = pe_guard()
            tevc = []
            for j in range(2):
                wev = [(tP, tk_piece[1] if j == 0 else gp_)]
                if t >= 2:
                    wev.append((tV, hist[("stt", t - 2, j)]))
                tevc.append(P.op("scalar", lambda a, sl=sl, j=j, b=banks[j]:
                                 a.copy(attb[:, sl * D + j * 512:
                                             sl * D + (j + 1) * 512], b),
                                 waits=wev))
                hist[("evc", t, j)] = tevc[j]
            if dbg:
                P.dma("gpsimd", d_att[t * 128:(t + 1) * 128, :],
                      attb[:, sl * D:(sl + 1) * D], waits=[(tA, tevc[1])], sem=dmp)
            if dbg2:
                P.dma("gpsimd", d_att2[t * 128:(t + 1) * 128, :],
                      attb[:, sl * D:(sl + 1) * D], waits=[(tA, tevc[1])], sem=dmp)
            wb1 = [(tA, tevc[0])]
            if t >= 2:
                wb1.append((tV, hist[("agg", t - 2)]))
            P.op("vector", lambda v, so=so, sl=sl: v.bn_stats(
                bnst[:, so * 12:so * 12 + 6], attb[:, sl * D:sl * D + 512]),
                waits=wb1)
            tb2 = P.op("vector", lambda v, so=so, sl=sl: v.bn_stats(
                bnst[:, so * 12 + 6:so * 12 + 12],
                attb[:, sl * D + 512:sl * D + 1024]),
                waits=[(tA, tevc[1])])
            wagg = [(tV, tb2)]
            if t >= 2:
                wagg.append((tA, hist[("ln", t - 2)]))
                wagg.append((tV, hist[("nm", t - 2)]))
            hist[("agg", t)] = P.op("vector", lambda v, so=so: v.bn_aggr(
                ln2s[:, so:so + 2], bnst[:, so * 12:so * 12 + 12]), waits=wagg)
            wln = [(tV, hist[("agg", t)])]
            if t >= 2:
                wln.append((tA, hist[("rs", t - 2)]))
            hist[("ln", t)] = P.op("scalar", lambda a, so=so: a.activation(
                ln2s[:, so + 2:so + 3], ln2s[:, so + 1:so + 2], AF.Ln,
                bias=epsb[:], scale=1.0), waits=wln)
            wrs = [(tA, hist[("ln", t)])]
            if t >= 2:
                wrs.append((tV, hist[("stt", t - 2, 1)]))
                wrs.append((tV, hist[("nm", t - 2)]))
            hist[("rs", t)] = P.op("scalar", lambda a, so=so: a.activation(
                ln2s[:, so + 3:so + 4], ln2s[:, so + 2:so + 3], AF.Exp,
                bias=0.0, scale=-0.5), waits=wrs)
            wnm = [(tA, hist[("rs", t)])]
            if t >= 2:
                wnm.append((tV, hist[("xa", t - 2)]))
            hist[("nm", t)] = P.op("vector", lambda v, so=so:
                                   v.scalar_tensor_tensor(
                ln2s[:, so + 4:so + 5], ln2s[:, so:so + 1], -1.0,
                ln2s[:, so + 3:so + 4], ALU.mult, ALU.mult), waits=wnm)
            wxa = [(xsl[sl], tlx), (tV, hist[("nm", t)])]
            if t >= 2:
                wxa.append((tV, hist[("stt", t - 2, 1)]))
            hist[("xa", t)] = P.op("vector", lambda v, sl=sl, so=so:
                                   v.tensor_scalar(
                xadj[:, sl * D:(sl + 1) * D], xst[:, sl * D:(sl + 1) * D],
                ln2s[:, so + 4:so + 5], None, ALU.add), waits=wxa)
            for j in range(2):
                hist[("stt", t, j)] = P.op(
                    "vector", lambda v, t=t, j=j, sl=sl, so=so:
                    v.scalar_tensor_tensor(
                        x2_sb[:, t * D + j * 512: t * D + (j + 1) * 512],
                        attb[:, sl * D + j * 512: sl * D + (j + 1) * 512],
                        ln2s[:, so + 3:so + 4],
                        xadj[:, sl * D + j * 512: sl * D + (j + 1) * 512],
                        ALU.mult, ALU.add),
                    waits=[(tV, hist[("xa", t)]), (tA, tevc[1])])
            t_x2.append(hist[("stt", t, 1)])

        if not full:
            t_st = []
            for t in range(NT):
                t_st.append(P.dma("gpsimd", y_out[t * 128:(t + 1) * 128, :],
                                  x2_sb[:, t * D:(t + 1) * D],
                                  waits=[(tV, t_x2[t])], sem=yst[t % 2]))
            gop(lambda g: g.nop(), waits=[(yst[0], t_st[-2]),
                                          (yst[1], t_st[-1])])
            block = e(nc.Block())
            P.emit(block)
            scC.close()
            return nc

        P.barrier()
        QAc = QA

        xqi = eC(nc.sbuf_tensor("xqi", [128, 2 * D], I16))
        xstats = eC(nc.sbuf_tensor("xstats", [128, NT], F32))
        xm_d = nc.dram_tensor("xm_d", [6, 128], F32).ap()
        scal_d = nc.dram_tensor("scal_d", [1, 16], F32).ap()

        t_sp = {}
        for t in range(NT):
            sl = t % 2
            c0 = t * D
            wb = [(tV, t_x2[t]), (tV, t_bn)]
            if t >= 2:
                wb.append((tV, hist[("ag3", t - 2)]))
            P.op("vector", lambda v, t=t, c0=c0: v.bn_stats(
                bnst[:, t * 12:t * 12 + 6], x2_sb[:, c0:c0 + 512]), waits=wb)
            tb = P.op("vector", lambda v, t=t, c0=c0: v.bn_stats(
                bnst[:, t * 12 + 6:t * 12 + 12], x2_sb[:, c0 + 512:c0 + 1024]))
            hist[("ag3", t)] = P.op("vector", lambda v, t=t: v.bn_aggr(
                mv_a[:, 2 * t:2 * t + 2], bnst[:, t * 12:t * 12 + 12]),
                waits=[(tV, tb)])
        t_ln3a = P.op("scalar", lambda a: a.activation(
            lntmp[:], mv_a[:, 1:2 * NT:2], AF.Ln, bias=epsb[:], scale=1.0),
            waits=[(tV, hist[("ag3", NT - 1)]), (tA, t_rstd)])
        t_ln3b = P.op("scalar", lambda a: a.activation(
            rstd_a[:], lntmp[:], AF.Exp, bias=0.0, scale=-0.5),
            waits=[(tA, t_ln3a)])
        t_nm3 = P.op("vector", lambda v: v.scalar_tensor_tensor(
            negmr_a[:], mv_a[:, 0:2 * NT:2], -1.0, rstd_a[:],
            ALU.mult, ALU.mult), waits=[(tA, t_ln3b), (tV, t_negmr)])
        for t in range(NT):
            sl = t % 2
            w = [(tV, t_x2[t])]
            if t >= 2:
                w.append((tV, hist[("ts3", t - 2)]))
            t_sp[t] = P.dma("gpsimd", dram["x2_d"][t * 128:(t + 1) * 128, :],
                            x2_sb[:, t * D:(t + 1) * D], waits=w, sem=x2s[sl])
            hist[("ts3", t)] = P.op("vector", lambda v, t=t: v.tensor_scalar(
                x2_sb[:, t * D:(t + 1) * D], x2_sb[:, t * D:(t + 1) * D],
                rstd_a[:, t:t + 1], negmr_a[:, t:t + 1], ALU.mult, ALU.add),
                waits=[(tV, t_nm3), (x2s[sl], t_sp[t])])
            hist[("xm", t)] = P.op("vector", lambda v, t=t: v.reduce_max(
                xstats.rearrange("p (a b) -> p a b", b=1)[:, t:t + 1, :],
                x2_sb[:, t * D:(t + 1) * D].rearrange("p (a b) -> p a b", b=D),
                AX.X, apply_absolute_value=True),
                waits=[(tV, hist[("ts3", t)])])
            drain_units(6000)
        t_xmax = P.op("vector", lambda v: v.reduce_max(
            lntmp.rearrange("p (a b) -> p a b", b=1)[:, 0:1, :],
            xstats.rearrange("p (a b) -> p a b", b=NT)[:, 0:1, :],
            AX.X, apply_absolute_value=False),
            waits=[(tV, hist[("xm", NT - 1)])])

        def xpart_max(row_idx, col_ap, w_tv, cell_out):
            tdo = P.dma("gpsimd", xm_d[row_idx:row_idx + 1, :].rearrange(
                "o p -> p o"), col_ap, waits=[(tV, w_tv)], sem=scl)
            tdi = P.dma("gpsimd", xm_row[:], xm_d[row_idx:row_idx + 1, :],
                        waits=[(scl, tdo)], sem=scl)
            return P.op("vector", lambda v, cell_out=cell_out: v.reduce_max(
                sc1[0:1, cell_out:cell_out + 1], xm_row[0:1, :], AX.X,
                apply_absolute_value=False), waits=[(scl, tdi)])

        def barrier_cc(k, in_cells, out_cells, w_tv):
            nv = len(in_cells)
            a0 = in_cells[0]
            tdo = P.dma("gpsimd", dram["cc_in"][k][0:1, 0:nv],
                        sc1[0:1, a0:a0 + nv],
                        waits=[(tV, w_tv), (ccz, 64)], sem=scl)
            tcc = P.op("gpsimd", lambda g, k=k: g.collective_compute(
                "AllReduce", ALU.max, replica_groups=[list(range(8))],
                ins=[dram["cc_in"][k]], outs=[dram["cc_out"][k]]),
                waits=[(scl, tdo)], n=1, tick=ccs[k])
            b0 = out_cells[0]
            tdi = P.dma("gpsimd", sc1[0:1, b0:b0 + nv],
                        dram["cc_out"][k][0:1, 0:nv],
                        waits=[(ccs[k], tcc)], sem=scl)
            return tdi

        def bcast_scalar(cell, dslot, bcol, w_prev):
            t1 = P.dma("gpsimd", scal_d[0:1, dslot:dslot + 1],
                       sc1[0:1, cell:cell + 1], waits=w_prev, sem=scl)
            t2 = P.dma("gpsimd", scale_bc[:, bcol:bcol + 1],
                       scal_d[0:1, dslot:dslot + 1].broadcast_to([128, 1]),
                       waits=[(scl, t1)], sem=scl)
            return t2

        t_g0 = xpart_max(0, lntmp[:, 0:1], t_xmax, 0)
        t_b0 = barrier_cc(0, [0], [1], t_g0)
        t_sx = P.op("vector", lambda v: v.tensor_scalar(
            sc1[0:1, 2:3], sc1[0:1, 1:2], 1.0 / QAc, 1e-12, ALU.mult, ALU.max),
            waits=[(scl, t_b0)])
        t_ivx = P.op("vector", lambda v: v.reciprocal(sc1[0:1, 3:4],
                                                      sc1[0:1, 2:3]),
                     waits=[(tV, t_sx)])
        t_bc0 = bcast_scalar(2, 0, 1, [(tV, t_ivx)])
        t_bc1 = bcast_scalar(3, 1, 0, [(scl, t_bc0)])

        t_xq = {}
        for t in range(NT):
            sl = t % 2
            w = [(scl, t_bc1), (tV, hist[("xm", t)])]
            if t >= 2:
                w.append((tV, t_xq[t - 2]))
            ti = P.op("vector", lambda v, t=t, sl=sl: v.tensor_scalar(
                xqi[:, sl * D:(sl + 1) * D], x2_sb[:, t * D:(t + 1) * D],
                scale_bc[:, 0:1], None, ALU.mult), waits=w)
            t_xq[t] = P.op("vector", lambda v, t=t, sl=sl: v.tensor_scalar(
                catT[:, t * D:(t + 1) * D], xqi[:, sl * D:(sl + 1) * D],
                scale_bc[:, 1:2], None, ALU.mult), waits=[(tV, ti)])
            drain_units(4000)

        prev_evq = 0
        t_trq = {}
        for t in range(NT):
            for half in range(2):
                w = [(tV, t_xq[t]), (tV, t_idn_bf), (tA, prev_evq)]
                for i4 in range(4):
                    i = half * 4 + i4
                    tk = P.op("tensor", lambda tt, t=t, i=i, i4=i4: tt.transpose(
                        ps_tr[:, i4 * 128:(i4 + 1) * 128],
                        catT[:, t * D + i * 128: t * D + (i + 1) * 128],
                        idnb[:]), waits=w)
                tk = pe_guard()
                prev_evq = P.op("scalar", lambda a, t=t, half=half: a.copy(
                    x_lnT.rearrange("p (i n) -> p i n", n=S)[
                        :, half * 4:half * 4 + 4, t * 128:(t + 1) * 128],
                    ps_tr.rearrange("p (i n) -> p i n", n=128)),
                    waits=[(tP, tk)])
                t_trq[(t, half)] = prev_evq
        t_xqT = prev_evq
        drain_units(10 ** 9)

        scC.close()
        P.barrier()

        scD = ExitStack()
        eD = scD.enter_context
        wdn_sb = eD(nc.sbuf_tensor("wdn_sb", [128, NIO * D], BF16))
        dstats = eD(nc.sbuf_tensor("dstats", [128, 100], F32))
        scP2 = ExitStack()
        eP2 = scP2.enter_context
        pnlb = eP2(nc.sbuf_tensor("pnlb", [128, 2 * ND * 128], BF16))
        gstage = eP2(nc.sbuf_tensor("gstage", [128, 2 * S], F32))
        grd = eP2(nc.sbuf_tensor("grd", [128, 2 * S], F32))
        gqi = eP2(nc.sbuf_tensor("gqi", [128, 2 * S], I16))
        sgf = eP2(nc.sbuf_tensor("sgf", [128, 2 * S], F32))
        u16t = eP2(nc.sbuf_tensor("u16t", [128, 2 * S], U16))
        g2st = eP2(nc.sbuf_tensor("g2st", [128, 2 * S], F32))
        X_full = [(x2s[0], x2s[0].v), (x2s[1], x2s[1].v)]
        deq_full = [(pol[0], pol[0].v), (pol[1], pol[1].v)]

        t_wdn = P.dma("sync", wdn_sb[:].rearrange("p (c n) -> p c n", n=D),
                      dram["wdnq_d"].rearrange("(c p) n -> p c n", p=128),
                      waits=deq_full, sem=wdl)

        ps4 = [ps_sA[:, 0:512], ps_sA[:, 512:1024],
               ps_sB[:, 0:512], ps_sB[:, 512:1024]]

        t_spill = {}
        t_dred = {}

        def upgate_loop(wname, outq_d, spill_d, scol, spsem, extra_unit=None):
            t_evc = {}
            pend_ev = [None]

            def prev_key(c2k):
                if c2k >= 0:
                    return (wname, c2k)
                if wname == "up":
                    return ("gt", NIO + c2k)
                return None

            def emit_evac(pe_tkt):
                (pc2, pj2, psl_) = pend_ev[0]
                wev = [(tP, pe_tkt)]
                pk = prev_key(pc2 - 2)
                if pk in t_spill:
                    pks = gsp if pk[0] == "gt" else usp
                    wev.append((pks[pk[1] % 2], t_spill[pk]))
                    wev.append((tV, t_dred[pk]))
                t_evc[(wname, pc2, pj2)] = P.op(
                    "scalar", lambda a, pc2=pc2, pj2=pj2, psl_=psl_: a.copy(
                        gstage[:, psl_ * S + pj2 * 512:
                               psl_ * S + (pj2 + 1) * 512],
                        ps4[pj2]), waits=wev)

            for c2 in range(NIO):
                sl = c2 % 2
                wpn = list(deq_full)
                pk = prev_key(c2 - 2)
                if pk in t_pnl_used:
                    wpn.append((tP, t_pnl_used[pk]))
                tpn = P.dma("sync",
                            pnlb[:, sl * ND * 128:(sl + 1) * ND * 128],
                            outq_d.rearrange("c p n -> p c n")[
                                :, :, 128 * c2:128 * (c2 + 1)],
                            waits=wpn, sem=pql[sl])
                for j2 in range(4):
                    wmm = [(pql[sl], tpn), (tA, t_xqT)]
                    if (wname, c2 - 1, j2) in t_evc:
                        wmm.append((tA, t_evc[(wname, c2 - 1, j2)]))
                    elif wname == "up" and c2 == 0:
                        wmm.append((tA, t_evc_gate_last[j2]))
                    for c_ in range(ND):
                        tk = P.op("tensor", lambda tt, c2=c2, j2=j2, c_=c_,
                                  sl=sl: tt.matmul(
                            ps4[j2],
                            pnlb[:, sl * ND * 128 + c_ * 128:
                                 sl * ND * 128 + (c_ + 1) * 128],
                            x_lnT[:, c_ * S + j2 * 512: c_ * S + (j2 + 1) * 512],
                            start=(c_ == 0), stop=(c_ == ND - 1)),
                            waits=wmm if c_ == 0 else ())
                    if pend_ev[0] is not None:
                        emit_evac(tk)
                    pend_ev[0] = (c2, j2, sl)
                t_pnl_used[(wname, c2)] = tP.v
                if (wname, c2 - 1, 3) in t_evc:
                    _red_spill(wname, c2 - 1, spill_d, scol, t_evc, spsem)
                if extra_unit is not None:
                    extra_unit(c2)
            gk = pe_guard()
            emit_evac(gk)
            _red_spill(wname, NIO - 1, spill_d, scol, t_evc, spsem)
            return t_evc

        def _red_spill(wname, c2, spill_d, scol, t_evc, spsem):
            sl = c2 % 2
            t_dred[(wname, c2)] = P.op(
                "vector", lambda v, c2=c2, sl=sl, scol=scol: v.reduce_max(
                    dstats.rearrange("p (a b) -> p a b", b=1)[
                        :, scol + c2:scol + c2 + 1, :],
                    gstage[:, sl * S:(sl + 1) * S].rearrange(
                        "p (a b) -> p a b", b=S),
                    AX.X, apply_absolute_value=True),
                waits=[(tA, t_evc[(wname, c2, 3)])])
            t_spill[(wname, c2)] = P.dma(
                "gpsimd", spill_d[c2 * 128:(c2 + 1) * 128, :],
                gstage[:, sl * S:(sl + 1) * S],
                waits=[(tA, t_evc[(wname, c2, 3)])], sem=spsem[sl])

        t_pnl_used = {}
        t_evc_gate_last = {}
        tg_evc = upgate_loop("gt", dram["wgtq_d"], dram["gate_d"], 0, gsp)
        for j2 in range(4):
            t_evc_gate_last[j2] = tg_evc[("gt", NIO - 1, j2)]
        G_full = [(gsp[0], gsp[0].v), (gsp[1], gsp[1].v)]

        tgf = P.op("vector", lambda v: v.reduce_max(
            dstats.rearrange("p (a b) -> p a b", b=1)[:, 96:97, :],
            dstats[:, 0:24].rearrange("p (a b) -> p a b", b=24), AX.X,
            apply_absolute_value=False), waits=[(tV, t_dred[("gt", NIO - 1)])])
        t_g1 = xpart_max(1, dstats[:, 96:97], tgf, 4)
        t_b1 = barrier_cc(1, [4], [5], t_g1)
        t_sg = P.op("vector", lambda v: v.tensor_scalar(
            sc1[0:1, 6:7], sc1[0:1, 5:6], 1.0 / QAc, 1e-12, ALU.mult, ALU.max),
            waits=[(scl, t_b1)])
        t_ivg = P.op("vector", lambda v: v.reciprocal(sc1[0:1, 7:8],
                                                      sc1[0:1, 6:7]),
                     waits=[(tV, t_sg)])
        tb = bcast_scalar(7, 2, 2, [(tV, t_ivg)])
        t_bc_g = bcast_scalar(6, 3, 3, [(scl, tb)])

        p2 = {}

        def pass2_unit(c2):
            if c2 >= NIO:
                return
            sl = c2 % 2
            w = list(G_full) + [(scl, t_bc_g)]
            if c2 >= 2:
                w.append((tV, p2[("g2", c2 - 2)]))
            trd = P.dma("gpsimd", grd[:, sl * S:(sl + 1) * S],
                        dram["gate_d"][c2 * 128:(c2 + 1) * 128, :],
                        waits=w, sem=grl[sl])
            w2 = [(grl[sl], trd)]
            if c2 >= 2:
                w2.append((tA, p2[("sig", c2 - 2)]))
            p2[("gi", c2)] = P.op("vector", lambda v, sl=sl: v.tensor_scalar(
                gqi[:, sl * S:(sl + 1) * S], grd[:, sl * S:(sl + 1) * S],
                scale_bc[:, 2:3], None, ALU.mult), waits=w2)
            w3 = [(tV, p2[("gi", c2)])]
            if c2 >= 2:
                w3.append((tV, p2[("u16", c2 - 2)]))
            p2[("sig", c2)] = P.op("scalar", lambda a, sl=sl: a.activation(
                sgf[:, sl * S:(sl + 1) * S], gqi[:, sl * S:(sl + 1) * S],
                AF.Sigmoid, bias=0.0, scale=scale_bc[:, 3:4]), waits=w3)
            w4 = [(tA, p2[("sig", c2)])]
            if c2 >= 2:
                w4.append((tV, p2[("g2", c2 - 2)]))
            p2[("u16", c2)] = P.op("vector", lambda v, sl=sl: v.tensor_scalar(
                u16t[:, sl * S:(sl + 1) * S], sgf[:, sl * S:(sl + 1) * S],
                65536.0, 65535.0, ALU.mult, ALU.min), waits=w4)
            w5 = [(tV, p2[("u16", c2)])]
            if c2 >= 2:
                w5.append((g2s[sl], p2[("sp", c2 - 2)]))
            p2[("g2", c2)] = P.op("vector", lambda v, sl=sl:
                                  v.scalar_tensor_tensor(
                g2st[:, sl * S:(sl + 1) * S], gqi[:, sl * S:(sl + 1) * S],
                1.0, u16t[:, sl * S:(sl + 1) * S], ALU.mult, ALU.mult),
                waits=w5)
            p2[("red", c2)] = P.op("vector", lambda v, c2=c2, sl=sl:
                                   v.reduce_max(
                dstats.rearrange("p (a b) -> p a b", b=1)[
                    :, 48 + c2:48 + c2 + 1, :],
                g2st[:, sl * S:(sl + 1) * S].rearrange("p (a b) -> p a b", b=S),
                AX.X, apply_absolute_value=True), waits=[(tV, p2[("g2", c2)])])
            p2[("sp", c2)] = P.dma("gpsimd",
                                   dram["g2u_d"][c2 * 128:(c2 + 1) * 128, :],
                                   g2st[:, sl * S:(sl + 1) * S],
                                   waits=[(tV, p2[("red", c2)])], sem=g2s[sl])

        upgate_loop("up", dram["wupq_d"], dram["up_d"], 24, usp,
                    extra_unit=pass2_unit)
        U_full = [(usp[0], usp[0].v), (usp[1], usp[1].v)]
        for c2 in range(NIO):
            if ("sp", c2) not in p2:
                pass2_unit(c2)
        F_full = [(g2s[0], g2s[0].v), (g2s[1], g2s[1].v)]

        tuf = P.op("vector", lambda v: v.reduce_max(
            dstats.rearrange("p (a b) -> p a b", b=1)[:, 97:98, :],
            dstats[:, 24:48].rearrange("p (a b) -> p a b", b=24),
            AX.X, apply_absolute_value=False),
            waits=[(tV, t_dred[("up", NIO - 1)])])
        tgf2 = P.op("vector", lambda v: v.reduce_max(
            dstats.rearrange("p (a b) -> p a b", b=1)[:, 98:99, :],
            dstats[:, 48:72].rearrange("p (a b) -> p a b", b=24),
            AX.X, apply_absolute_value=False),
            waits=[(tV, p2[("red", NIO - 1)])])
        t_u8 = xpart_max(2, dstats[:, 97:98], tuf, 8)
        t_u9 = xpart_max(3, dstats[:, 98:99], max(tgf2, t_u8), 9)
        t_b2 = barrier_cc(2, [8, 9], [10, 11], t_u9)
        t_su = P.op("vector", lambda v: v.tensor_scalar(
            sc1[0:1, 12:13], sc1[0:1, 10:11], 1.0 / QAc, 1e-12,
            ALU.mult, ALU.max), waits=[(scl, t_b2)])
        t_ivu = P.op("vector", lambda v: v.reciprocal(
            sc1[0:1, 13:14], sc1[0:1, 12:13]), waits=[(tV, t_su)])
        t_t1 = P.op("vector", lambda v: v.tensor_tensor(
            sc1[0:1, 16:17], sc1[0:1, 11:12], sc1[0:1, 6:7], ALU.mult),
            waits=[(scl, t_b2), (tV, t_sg)])
        t_sg2 = P.op("vector", lambda v: v.tensor_scalar(
            sc1[0:1, 14:15], sc1[0:1, 16:17], 2.0 ** -16 / QAc, 1e-12,
            ALU.mult, ALU.max), waits=[(tV, t_t1)])
        t_r1a = P.op("vector", lambda v: v.reciprocal(
            sc1[0:1, 22:23], sc1[0:1, 14:15]), waits=[(tV, t_sg2)])
        t_r1b = P.op("vector", lambda v: v.tensor_tensor(
            sc1[0:1, 23:24], sc1[0:1, 22:23], sc1[0:1, 6:7], ALU.mult),
            waits=[(tV, t_r1a)])
        t_r1 = P.op("vector", lambda v: v.tensor_scalar(
            sc1[0:1, 15:16], sc1[0:1, 23:24], 2.0 ** -16, None, ALU.mult),
            waits=[(tV, t_r1b)])
        tb = bcast_scalar(13, 4, 4, [(tV, t_ivu), (tV, t_r1)])
        t_bc_p3 = bcast_scalar(15, 5, 5, [(scl, tb)])

        scP2.close()
        scP3 = ExitStack()
        eP3 = scP3.enter_context
        urd = eP3(nc.sbuf_tensor("urd", [128, 2 * S], F32))
        g2rd = eP3(nc.sbuf_tensor("g2rd", [128, 2 * S], F32))
        upi = eP3(nc.sbuf_tensor("upi", [128, 2 * S], I16))
        g2i = eP3(nc.sbuf_tensor("g2i", [128, 2 * S], I16))
        oust = eP3(nc.sbuf_tensor("oust", [128, 2 * S], F32))
        p3 = {}
        for c2 in range(NIO):
            sl = c2 % 2
            w = list(U_full) + [(scl, t_bc_p3)]
            if c2 >= 2:
                w.append((tV, p3[("ou", c2 - 2)]))
            tru = P.dma("sync", urd[:, sl * S:(sl + 1) * S],
                        dram["up_d"][c2 * 128:(c2 + 1) * 128, :], waits=w,
                        sem=url[sl])
            trg = P.dma("sync", g2rd[:, sl * S:(sl + 1) * S],
                        dram["g2u_d"][c2 * 128:(c2 + 1) * 128, :],
                        waits=list(F_full) +
                        ([(tV, p3[("ou", c2 - 2)])] if c2 >= 2 else []),
                        sem=g2l[sl])
            w2 = [(url[sl], tru)]
            if c2 >= 2:
                w2.append((tV, p3[("ou", c2 - 2)]))
            p3[("ui", c2)] = P.op("vector", lambda v, sl=sl: v.tensor_scalar(
                upi[:, sl * S:(sl + 1) * S], urd[:, sl * S:(sl + 1) * S],
                scale_bc[:, 4:5], None, ALU.mult), waits=w2)
            p3[("gi", c2)] = P.op("vector", lambda v, sl=sl: v.tensor_scalar(
                g2i[:, sl * S:(sl + 1) * S], g2rd[:, sl * S:(sl + 1) * S],
                scale_bc[:, 5:6], None, ALU.mult),
                waits=[(g2l[sl], trg), (tV, p3[("ui", c2)])])
            p3[("ou", c2)] = P.op("vector", lambda v, sl=sl:
                                  v.scalar_tensor_tensor(
                oust[:, sl * S:(sl + 1) * S], upi[:, sl * S:(sl + 1) * S],
                1.0, g2i[:, sl * S:(sl + 1) * S], ALU.mult, ALU.mult),
                waits=[(tV, p3[("gi", c2)])] +
                ([(ous[sl], p3[("sp", c2 - 2)])] if c2 >= 2 else []))
            p3[("red", c2)] = P.op("vector", lambda v, c2=c2, sl=sl:
                                   v.reduce_max(
                dstats.rearrange("p (a b) -> p a b", b=1)[
                    :, 72 + c2:72 + c2 + 1, :],
                oust[:, sl * S:(sl + 1) * S].rearrange("p (a b) -> p a b", b=S),
                AX.X, apply_absolute_value=True), waits=[(tV, p3[("ou", c2)])])
            p3[("sp", c2)] = P.dma("gpsimd",
                                   dram["ou_d"][c2 * 128:(c2 + 1) * 128, :],
                                   oust[:, sl * S:(sl + 1) * S],
                                   waits=[(tV, p3[("red", c2)])], sem=ous[sl])

        O_full = [(ous[0], ous[0].v), (ous[1], ous[1].v)]
        tof = P.op("vector", lambda v: v.reduce_max(
            dstats.rearrange("p (a b) -> p a b", b=1)[:, 99:100, :],
            dstats[:, 72:96].rearrange("p (a b) -> p a b", b=24),
            AX.X, apply_absolute_value=False),
            waits=[(tV, p3[("red", NIO - 1)])])
        t_o17 = xpart_max(4, dstats[:, 99:100], tof, 17)
        t_b3 = barrier_cc(3, [17], [18], t_o17)
        t_q1 = P.op("vector", lambda v: v.tensor_tensor(
            sc1[0:1, 16:17], sc1[0:1, 18:19], sc1[0:1, 14:15], ALU.mult),
            waits=[(scl, t_b3), (tV, t_sg2)])
        t_q2 = P.op("vector", lambda v: v.tensor_tensor(
            sc1[0:1, 22:23], sc1[0:1, 16:17], sc1[0:1, 12:13], ALU.mult),
            waits=[(tV, t_q1)])
        t_so = P.op("vector", lambda v: v.tensor_scalar(
            sc1[0:1, 19:20], sc1[0:1, 22:23], 1.0 / QAc, 1e-12,
            ALU.mult, ALU.max), waits=[(tV, t_q2)])
        t_q3 = P.op("vector", lambda v: v.reciprocal(
            sc1[0:1, 16:17], sc1[0:1, 19:20]), waits=[(tV, t_so)])
        t_q4 = P.op("vector", lambda v: v.tensor_tensor(
            sc1[0:1, 23:24], sc1[0:1, 14:15], sc1[0:1, 12:13], ALU.mult),
            waits=[(tV, t_q3)])
        t_r3 = P.op("vector", lambda v: v.tensor_tensor(
            sc1[0:1, 20:21], sc1[0:1, 23:24], sc1[0:1, 16:17], ALU.mult),
            waits=[(tV, t_q4)])
        t_so2 = P.op("vector", lambda v: v.tensor_tensor(
            sc1[0:1, 21:22], sc1[0:1, 19:20], sc1[0:1, 19:20], ALU.mult),
            waits=[(tV, t_r3)])
        tb = bcast_scalar(20, 6, 6, [(tV, t_so2)])
        tb = bcast_scalar(19, 7, 7, [(scl, tb)])
        t_bc_p4 = bcast_scalar(21, 8, 8, [(scl, tb)])

        scP3.close()
        scP4 = ExitStack()
        eP4 = scP4.enter_context
        obf3 = eP4(nc.sbuf_tensor("obf3", [128, 8 * S], BF16))
        ordt = eP4(nc.sbuf_tensor("ordt", [128, 2 * S], F32))
        oi16 = eP4(nc.sbuf_tensor("oi16", [128, 2 * S], I16))
        ystage = eP4(nc.sbuf_tensor("ystage", [128, 2 * D], F32))
        x2rd = eP4(nc.sbuf_tensor("x2rd", [128, 2 * D], F32))
        yadj = eP4(nc.sbuf_tensor("yadj", [128, 2 * D], F32))
        youts = eP4(nc.sbuf_tensor("youts", [128, 2 * D], F32))
        ln4s = eP4(nc.sbuf_tensor("ln4s", [128, 14], F32))

        def obf(c2):
            if c2 < 8:
                return catT[:, c2 * S:(c2 + 1) * S]
            if c2 < 16:
                return x_lnT[:, (c2 - 8) * S:(c2 - 7) * S]
            return obf3[:, (c2 - 16) * S:(c2 - 15) * S]

        p4 = {}
        for c2 in range(NIO):
            sl = c2 % 2
            w = list(O_full) + [(scl, t_bc_p4)]
            if c2 >= 2:
                w.append((tV, p4[("bf", c2 - 2)]))
            tro = P.dma("sync", ordt[:, sl * S:(sl + 1) * S],
                        dram["ou_d"][c2 * 128:(c2 + 1) * 128, :], waits=w,
                        sem=orl[sl])
            w2 = [(orl[sl], tro)]
            if c2 >= 2:
                w2.append((tV, p4[("bf", c2 - 2)]))
            p4[("i", c2)] = P.op("vector", lambda v, sl=sl: v.tensor_scalar(
                oi16[:, sl * S:(sl + 1) * S], ordt[:, sl * S:(sl + 1) * S],
                scale_bc[:, 6:7], None, ALU.mult), waits=w2)
            p4[("bf", c2)] = P.op("vector", lambda v, c2=c2, sl=sl:
                                  v.tensor_scalar(
                obf(c2), oi16[:, sl * S:(sl + 1) * S], 1.0, None, ALU.mult),
                waits=[(tV, p4[("i", c2)])])

        hist4 = {}
        t_yst = []
        for t in range(NT):
            sl = t % 2
            so = sl * 7
            banks = ((ps_o1[:, :], ps_o2[:, :]) if sl == 0
                     else (ps_sA[:, 0:512], ps_sA[:, 512:1024]))
            wx = list(X_full)
            if t >= 2:
                wx.append((tV, hist4[("stt", t - 2, 1)]))
            tlx = P.dma("sync", x2rd[:, sl * D:(sl + 1) * D],
                        dram["x2_d"][t * 128:(t + 1) * 128, :], waits=wx,
                        sem=x2r[sl])
            tk_piece = []
            for j in range(2):
                w2 = [(wdl, t_wdn)]
                if t >= 2:
                    w2.append((tA, hist4[("evc", t - 2, j)]))
                elif sl == 1:
                    pass
                for c2 in range(NIO):
                    wmm = list(w2) + [(tV, p4[("bf", c2)])]
                    tk = P.op("tensor", lambda tt, t=t, j=j, c2=c2,
                              b=banks[j]: tt.matmul(
                        b, obf(c2)[:, t * 128:(t + 1) * 128],
                        wdn_sb[:, c2 * D + j * 512: c2 * D + (j + 1) * 512],
                        start=(c2 == 0), stop=(c2 == NIO - 1)),
                        waits=wmm if c2 == 0 else [(tV, p4[("bf", c2)])])
                tk_piece.append(tk)
            gp_ = pe_guard()
            tevc = []
            for j in range(2):
                wev = [(tP, tk_piece[1] if j == 0 else gp_)]
                if t >= 2:
                    wev.append((tV, hist4[("stt", t - 2, j)]))
                tevc.append(P.op("scalar", lambda a, sl=sl, j=j, b=banks[j]:
                                 a.copy(ystage[:, sl * D + j * 512:
                                               sl * D + (j + 1) * 512], b),
                                 waits=wev))
                hist4[("evc", t, j)] = tevc[j]
            wb1 = [(tA, tevc[0])]
            if t >= 2:
                wb1.append((tV, hist4[("agg", t - 2)]))
            P.op("vector", lambda v, so=so, sl=sl: v.bn_stats(
                bnst[:, so * 12:so * 12 + 6], ystage[:, sl * D:sl * D + 512]),
                waits=wb1)
            tb2 = P.op("vector", lambda v, so=so, sl=sl: v.bn_stats(
                bnst[:, so * 12 + 6:so * 12 + 12],
                ystage[:, sl * D + 512:sl * D + 1024]),
                waits=[(tA, tevc[1])])
            wagg = [(tV, tb2)]
            if t >= 2:
                wagg.append((tA, hist4[("ln", t - 2)]))
                wagg.append((tV, hist4[("nm", t - 2)]))
            hist4[("agg", t)] = P.op("vector", lambda v, so=so: v.bn_aggr(
                ln4s[:, so:so + 2], bnst[:, so * 12:so * 12 + 12]), waits=wagg)
            wln = [(tV, hist4[("agg", t)]), (scl, t_bc_p4)]
            if t >= 2:
                wln.append((tA, hist4[("rs", t - 2)]))
            hist4[("ln", t)] = P.op("scalar", lambda a, so=so: a.activation(
                ln4s[:, so + 2:so + 3], ln4s[:, so + 1:so + 2], AF.Ln,
                bias=epsb[:], scale=scale_bc[:, 8:9]), waits=wln)
            wrs = [(tA, hist4[("ln", t)])]
            if t >= 2:
                wrs.append((tV, hist4[("srr", t - 2)]))
            hist4[("rs", t)] = P.op("scalar", lambda a, so=so: a.activation(
                ln4s[:, so + 3:so + 4], ln4s[:, so + 2:so + 3], AF.Exp,
                bias=0.0, scale=-0.5), waits=wrs)
            wsr = [(tA, hist4[("rs", t)])]
            if t >= 2:
                wsr.append((tV, hist4[("nm", t - 2)]))
            hist4[("srr", t)] = P.op("vector", lambda v, so=so: v.tensor_scalar(
                ln4s[:, so + 4:so + 5], ln4s[:, so + 3:so + 4],
                scale_bc[:, 7:8], None, ALU.mult), waits=wsr)
            wnm = [(tV, hist4[("srr", t)])]
            if t >= 2:
                wnm.append((tV, hist4[("xa", t - 2)]))
            hist4[("nm", t)] = P.op("vector", lambda v, so=so:
                                    v.scalar_tensor_tensor(
                ln4s[:, so + 5:so + 6], ln4s[:, so:so + 1], -1.0,
                ln4s[:, so + 4:so + 5], ALU.mult, ALU.mult), waits=wnm)
            wxa = [(x2r[sl], tlx), (tV, hist4[("nm", t)])]
            if t >= 2:
                wxa.append((tV, hist4[("stt", t - 2, 1)]))
            hist4[("xa", t)] = P.op("vector", lambda v, sl=sl, so=so:
                                    v.tensor_scalar(
                yadj[:, sl * D:(sl + 1) * D], x2rd[:, sl * D:(sl + 1) * D],
                ln4s[:, so + 5:so + 6], None, ALU.add), waits=wxa)
            for j in range(2):
                wst_ = [(tV, hist4[("xa", t)]), (tA, tevc[j])]
                if t >= 2:
                    wst_.append((yst[sl], t_yst[t - 2]))
                hist4[("stt", t, j)] = P.op(
                    "vector", lambda v, t=t, j=j, sl=sl, so=so:
                    v.scalar_tensor_tensor(
                        youts[:, sl * D + j * 512: sl * D + (j + 1) * 512],
                        ystage[:, sl * D + j * 512: sl * D + (j + 1) * 512],
                        ln4s[:, so + 4:so + 5],
                        yadj[:, sl * D + j * 512: sl * D + (j + 1) * 512],
                        ALU.mult, ALU.add), waits=wst_)
            t_yst.append(P.dma("gpsimd", y_out[t * 128:(t + 1) * 128, :],
                               youts[:, sl * D:(sl + 1) * D],
                               waits=[(tV, hist4[("stt", t, 1)])],
                               sem=yst[sl]))
        gop(lambda g: g.nop(), waits=[(yst[0], t_yst[-2]),
                                      (yst[1], t_yst[-1])])
        block = e(nc.Block())
        P.emit(block)
        scP4.close()
        scD.close()
    return nc



def _layout_inputs(w_q, w_k, w_v, w_o, w_gate, w_up, w_down):
    C = np.ascontiguousarray
    return {
        "wq_t": C(w_q.transpose(2, 0, 1).reshape(D, D)),
        "wk_t": C(w_k.transpose(2, 0, 1).reshape(D, D)),
        "wv_t": C(w_v.transpose(2, 0, 1).reshape(D, D)),
        "wo_t": C(w_o.T),
        "wup_n": C(w_up), "wup_t": C(w_up.T),
        "wgt_n": C(w_gate), "wgt_t": C(w_gate.T),
        "wdn_n": C(w_down), "wdn_t": C(w_down.T),
    }


def kernel(L0_attn_x, L0_attn_w_q, L0_attn_w_k, L0_attn_w_v, L0_attn_w_o,
           L0_ffn_w_gate, L0_ffn_w_up, L0_ffn_w_down):
    x = np.ascontiguousarray(np.asarray(L0_attn_x, dtype=np.float32))
    ws = _layout_inputs(np.asarray(L0_attn_w_q, np.float32),
                        np.asarray(L0_attn_w_k, np.float32),
                        np.asarray(L0_attn_w_v, np.float32),
                        np.asarray(L0_attn_w_o, np.float32),
                        np.asarray(L0_ffn_w_gate, np.float32),
                        np.asarray(L0_ffn_w_up, np.float32),
                        np.asarray(L0_ffn_w_down, np.float32))
    nc = build("full")
    in_maps = [dict(ws, x=np.ascontiguousarray(x[c])) for c in range(8)]
    res = run_bass_kernel_spmd(nc, in_maps, core_ids=list(range(8)))
    return np.stack([res.results[c]["y"] for c in range(8)], axis=0)
